# version 01
# baseline (speedup 1.0000x reference)
import numpy as np
import ml_dtypes
from contextlib import ExitStack

import concourse.bass as bass
import concourse.mybir as mybir
from concourse.bass_utils import run_bass_kernel_spmd

F32 = mybir.dt.float32
BF16 = mybir.dt.bfloat16
I32 = mybir.dt.int32
AF = mybir.ActivationFunctionType
ALU = mybir.AluOpType
AX = mybir.AxisListType

NCORE = 8
B, S, D = 4, 4096, 1024
TOK = 2048
NT = TOK // 128
HEADS, DH, LM = 8, 64, 256
PITER = 6
DFF = 4096
NEXP = 8
BF = ml_dtypes.bfloat16


def _dup_split(rows, cols, vals):
    keep = (rows >= 0) & (rows < S)
    rows = rows[keep]
    cols = np.clip(cols[keep], 0, S - 1)
    vals = vals[keep]
    last = cols == (S - 1)
    lr, lv = rows[last], vals[last]
    cnt = np.zeros(S, np.int64)
    np.add.at(cnt, lr, 1)
    mpad = max(4, int(((cnt.max() + 3) // 4) * 4))
    CV = np.zeros((S, mpad), dtype=np.float32)
    pos = np.zeros(S, np.int64)
    for r, vv in zip(lr, lv):
        CV[r, pos[r]] = vv
        pos[r] += 1
    rows, cols, vals = rows[~last], cols[~last], vals[~last]
    E = rows.shape[0]
    key = rows.astype(np.int64) * S + cols.astype(np.int64)
    _, first_idx = np.unique(key, return_index=True)
    mask = np.zeros(E, dtype=bool)
    mask[first_idx] = True
    AT = np.zeros((S, S), dtype=np.float32)
    AT[cols[mask], rows[mask]] = vals[mask]
    dup = ~mask
    nd = int(dup.sum())
    ndp = max(128, ((nd + 127) // 128) * 128)
    A2cT = np.zeros((S, ndp), dtype=np.float32)
    Pm = np.zeros((ndp, S), dtype=np.float32)
    if nd:
        A2cT[cols[dup], np.arange(nd)] = vals[dup]
        Pm[np.arange(nd), rows[dup]] = 1.0
    return AT, A2cT, Pm, ndp, CV, mpad


def _conv_bands(conv_w):
    w = np.asarray(conv_w)[:, 0, :, 0]
    CB = np.zeros((HEADS, 3, 128, 128), dtype=np.float32)
    a = np.arange(128)[:, None]
    bb = np.arange(128)[None, :]
    for d in range(3):
        t = 128 * (d - 1) + a - bb + 16
        m = (t >= 0) & (t <= 32)
        for h in range(HEADS):
            blk = np.zeros((128, 128), dtype=np.float32)
            blk[m] = w[h][t[m]]
            CB[h, d] = blk
    return CB


class Ctr:

    def __init__(self, entry, base):
        self.e = entry
        self.h = entry[0]
        self.b = base

    @property
    def n(self):
        return self.e[1] - self.b

    def inc(self, instr, k=1):
        instr.then_inc(self.h, k)
        self.e[1] += k
        return self.e[1] - self.b

    def dinc(self, instr):
        return self.inc(instr, 16)

    def wait(self, eng, val=None):
        eng.wait_ge(self.h, self.e[1] if val is None else self.b + val)


def build(ndp, mpad, taps=(), upto=99):
    nc = bass.Bass("TRN2", target_bir_lowering=False)
    dram = {}

    def din(name, shape, dtype=F32):
        dram[name] = nc.dram_tensor(name, shape, dtype, kind="ExternalInput")
        return dram[name]

    x0_d = din("x0", [S, D])
    xo_d = din("xown", [TOK, D])
    wbig_d = din("wbig", [D, 2048])
    wv_d = din("wv", [D, D])
    wvb_d = din("wvb", [1, D])
    outw_d = din("outw", [512, D])
    outb_d = din("outb", [128, 8])
    cb_d = din("cb", [HEADS * 3 * 128, 128])
    wrt_d = din("wrt", [D, 8])
    at_d = din("at", [S, S])
    a2ct_d = din("a2ct", [S, ndp])
    pm_d = din("pm", [ndp, S])
    cv_d = din("cv", [S, mpad])
    w1t_d = din("w1t", [NEXP, D, DFF])
    w3t_d = din("w3t", [NEXP, D, DFF])
    w2t_d = din("w2t", [NEXP, DFF, D])
    identb_d = din("identb", [128, 128], BF16)
    identf_d = din("identf", [128, 128])
    onesb_d = din("onesb", [128, 128], BF16)
    onesf_d = din("onesf", [128, 128])
    m0m1_d = din("m0m1", [128, 256], BF16)
    c715_d = din("c715", [128, 3 * 512])
    mks_d = din("mks", [128, 2])

    out_d = nc.dram_tensor("out", [128, NT, D], F32, kind="ExternalOutput")
    p1_hbm = nc.dram_tensor("p1spill", [128, 32, 512], BF16, kind="Internal")
    xg0n_hbm = nc.dram_tensor("xg0nspill", [128, NT, 1024], BF16, kind="Internal")

    TAPS = {
        "t_qkT": [128, 1024], "t_v": [128, 512], "t_att2": [128, 1024],
        "t_scale": [128, 16], "t_ZT": [128, 1024], "t_C": [128, 256],
        "t_B": [128, 256], "t_comb": [128, 1024], "t_xg0T": [128, 1024],
        "t_q0k0": [128, 512], "t_araw": [128, 32], "t_alpha": [128, 32],
        "t_x1": [128, 1024], "t_y": [128, 1024], "t_gates": [128, 128],
        "t_ql": [128, 1024], "t_e1": [128, 1024], "t_e3": [128, 1024],
        "t_xg0own": [128, 1024], "t_kT": [128, 1024], "t_lgt": [128, 128],
    }
    tap_d = {t: nc.dram_tensor(t, TAPS[t], F32, kind="ExternalOutput")
             for t in taps}

    class _Cut(Exception):
        pass

    def maybe_cut(n):
        if upto <= n:
            raise _Cut

    ctx = ExitStack()
    _stub = {}
    try:
      with ctx:
        _ns = [0]
        _pool = []

        def mkctr(name=None):
            h = ctx.enter_context(nc.semaphore(name or ("c%d" % (_ns[0]))))
            _ns[0] += 1
            entry = [h, 0]
            if name is None:
                _pool.append(entry)
            return Ctr(entry, 0)

        _rr = [0]

        def fresh():
            if len(_pool) < 40:
                h = ctx.enter_context(nc.semaphore("p%d" % len(_pool)))
                _pool.append([h, 0])
                entry = _pool[-1]
            else:
                entry = _pool[_rr[0] % len(_pool)]
                _rr[0] += 1
            return Ctr(entry, entry[1])

        def sb(name, shape, dtt):
            return ctx.enter_context(nc.sbuf_tensor(name, shape, dtt))

        _uq = [0]

        def un(nm):
            _uq[0] += 1
            return "%s_%d" % (nm, _uq[0])

        c_tap = mkctr("tap")
        c_st = mkctr("st")
        tapbufs = {t: ctx.enter_context(
            nc.sbuf_tensor("tb_" + t, TAPS[t], F32)) for t in tap_d}

        def tap(blk, name, make_ap):
            if name not in tap_d:
                return
            tb = tapbufs[name]
            want = [0]

            def _vv(v):
                want[0] = c_tap.inc(v.tensor_copy(tb[:], make_ap()))

            def _sy(sync):
                c_tap.wait(sync, want[0])
                c_st.dinc(sync.dma_start(tap_d[name][:], tb[:]))

            blk.vector(_vv)
            blk.sync(_sy)

        def emit_taps(pairs):
            pairs = [(n, f) for n, f in pairs if n in tap_d]
            if not pairs:
                return
            with nc.Block() as blk:
                for n, f in pairs:
                    tap(blk, n, f)

        sc_wb = ExitStack()
        wbig_b = sc_wb.enter_context(nc.sbuf_tensor("wbig_b", [128, 8, 2048], BF16,
                                                    side="right"))
        identb = sb("identb_s", [128, 128], BF16)
        identf = sb("identf_s", [128, 128], F32)
        onesb = sb("onesb_s", [128, 128], BF16)
        onesf = sb("onesf_s", [128, 128], F32)
        onescol = sb("onescol_s", [128, 1], BF16)
        m0m1 = sb("m0m1_s", [128, 256], BF16)
        outb_t = sb("outb_s", [128, 8], F32)
        wrt_f = sb("wrtf_s", [128, 8, 8], F32)
        wrt_hi = sb("wrth_s", [128, 8, 8], BF16)
        wrt_lo = sb("wrtl_s", [128, 8, 8], BF16)
        eps_t = sb("eps_s", [128, 1], F32)
        negone = sb("negone_s", [128, 1], F32)
        sqrt2 = sb("sqrt2_s", [128, 1], F32)
        neghalf = sb("neghalf_s", [128, 1], F32)
        mks_s = sb("mks_sb", [128, 2], F32)
        gates_e = sb("gates_s", [128, 8, NT], F32)
        alpha_own = sb("alphao_s", [128, NT], F32)
        ro_st = sb("rost_s", [128, NT], F32)
        no_st = sb("nost_s", [128, NT], F32)
        araw = sb("araw_s", [128, 32], F32)
        stub1 = sb("stub1", [128, 1], F32)
        alpha_sb = sb("alpha_s", [128, 32], F32)

        if upto < 99:
            _stub["z"] = sb("zstub", [128, 1024], BF16)
            _stub["c"] = mkctr("stub")
        c_ld = mkctr("ld")
        c_cw = mkctr("cw")
        c_sx = mkctr("sx")
        c_gd = mkctr("gd")
        c_xo = mkctr("xo")
        c_vsm = mkctr("vsm")

        with nc.Block() as blk:
            def _sync(sync):
                for tens, dd in [(identf, identf_d), (onesf, onesf_d),
                                 (outb_t, outb_d),
                                 (mks_s, mks_d)]:
                    c_ld.dinc(sync.dma_start(tens[:], dd[:]))
                c_ld.dinc(sync.dma_start(
                    bass.AP(wrt_f, 0, [[64, 128], [8, 8], [1, 8]]),
                    bass.AP(wrt_d, 0, [[8, 128], [128 * 8, 8], [1, 8]])))

            def _gp(g):
                for tens, dd in [(identb, identb_d), (onesb, onesb_d),
                                 (m0m1, m0m1_d)]:
                    c_cw.dinc(g.dma_start(tens[:], dd[:]))
                c_cw.dinc(g.dma_start(
                    bass.AP(wbig_b, 0, [[8 * 2048, 128], [2048, 8], [1, 2048]]),
                    bass.AP(wbig_d, 0, [[2048, 128], [128 * 2048, 8], [1, 2048]])))

            ldw = c_ld.n + 5 * 16
            cww = c_cw.n + 4 * 16

            def _vec(v, ldw=ldw, cww=cww):
                v.memset(eps_t[:], 1e-5)
                v.memset(negone[:], -1.0)
                v.memset(sqrt2[:], 1.4142135623730951)
                v.memset(neghalf[:], -0.5)
                c_cw.wait(v, cww)
                v.tensor_copy(onescol[:], onesb[:, 0:1])
                c_ld.wait(v, ldw)
                w = c_sx.inc(v.tensor_copy(wrt_hi[:], wrt_f[:]))
                c_sx.wait(v, w)
                v.tensor_sub(wrt_lo[:], wrt_f[:], wrt_hi[:])

            blk.sync(_sync)
            blk.gpsimd(_gp)
            blk.vector(_vec)

        sc_a = ExitStack()
        cb_b = sc_a.enter_context(nc.sbuf_tensor("cb_s", [128, 24, 128], BF16))
        d3col = sc_a.enter_context(nc.sbuf_tensor("d3col", [128, 2], F32))
        C_b = sc_a.enter_context(nc.sbuf_tensor("C_b", [128, 8, 128], BF16))
        B_b = sc_a.enter_context(nc.sbuf_tensor("B_b", [128, 8, 128], BF16))
        qT = sc_a.enter_context(nc.sbuf_tensor("qT", [128, 4, S], BF16))
        v_sb = sc_a.enter_context(nc.sbuf_tensor("v_sb", [128, 32, 512], BF16))
        qklT = sc_a.enter_context(nc.sbuf_tensor("qklT", [128, 8, LM], BF16))
        sc_k = ExitStack()
        kT_sb = sc_k.enter_context(nc.sbuf_tensor("kT", [128, 4, S], BF16))

        s2 = ExitStack()
        x0st = s2.enter_context(nc.sbuf_tensor("x0st", [128, 2, 2, 1024], BF16))
        lnring = s2.enter_context(nc.sbuf_tensor("lnring", [128, 2, 1024], BF16))
        lnT_h = s2.enter_context(nc.sbuf_tensor("lnT_h", [128, 8, 2048], BF16))
        sqdump = s2.enter_context(nc.sbuf_tensor("sqdump", [128, 1024], BF16))
        stt_ = s2.enter_context(nc.sbuf_tensor("stt_", [128, 12], F32))
        p1ring = s2.enter_context(nc.sbuf_tensor("p1ring", [128, 2, 512], BF16))

        def emit_ln(blk, src_tile, n_tiles, c_ldpre, ld_base, ld_step,
                    out_ring, st, sq_buf, c_ap, c_tp, c_sq, c_sr, c_vx,
                    save_stats=None, ld_ctr=None):
            nch = n_tiles // 2

            def _scal(sc):
                for ch in range(nch):
                    if ld_ctr is not None:
                        sc.wait_ge(ld_ctr.h, ld_base + (ch + 1) * ld_step)
                    if ch >= 1:
                        c_vx.wait(sc, ch * 5 - 3)
                    sc.activation(sq_buf[:], src_tile(ch * 2), AF.Square,
                                  accum_out=st[:, 2:3])
                    c_sq.inc(sc.activation(sq_buf[:], src_tile(ch * 2 + 1),
                                           AF.Square, accum_out=st[:, 3:4]))
                    c_vx.wait(sc, ch * 5 + 4)
                    w = c_sr.inc(sc.activation(st[:, 6:8], st[:, 4:6],
                                               AF.Ln, bias=eps_t[:]))
                    c_sr.wait(sc, w)
                    c_sr.inc(sc.activation(st[:, 8:10], st[:, 6:8], AF.Exp,
                                           scale=neghalf[:, 0:1]))

            def _vec(v):
                for ch in range(nch):
                    if ld_ctr is not None:
                        v.wait_ge(ld_ctr.h, ld_base + (ch + 1) * ld_step)
                    v.tensor_reduce(st[:, 0:1], src_tile(ch * 2), AX.X, ALU.add)
                    w = c_vx.inc(v.tensor_reduce(st[:, 1:2], src_tile(ch * 2 + 1),
                                                 AX.X, ALU.add))
                    c_vx.wait(v, w)
                    c_sq.wait(v, ch + 1)
                    v.tensor_scalar_mul(st[:, 0:2], st[:, 0:2], 1.0 / D)
                    w = c_vx.inc(v.tensor_scalar_mul(st[:, 4:6], st[:, 2:4],
                                                     1.0 / D))
                    c_vx.wait(v, w)
                    w = c_vx.inc(v.tensor_mul(st[:, 6:8], st[:, 0:2], st[:, 0:2]))
                    c_vx.wait(v, w)
                    w = c_vx.inc(v.tensor_sub(st[:, 4:6], st[:, 4:6], st[:, 6:8]))
                    c_vx.wait(v, w)
                    c_sr.wait(v, 2 * (ch + 1))
                    w = c_vx.inc(v.scalar_tensor_tensor(st[:, 10:12], st[:, 0:2],
                                                        -1.0, st[:, 8:10],
                                                        ALU.mult, ALU.mult))
                    c_vx.wait(v, w)
                    if save_stats is not None:
                        rs, ns = save_stats
                        v.tensor_copy(rs[:, ch * 2:ch * 2 + 2], st[:, 8:10])
                        v.tensor_copy(ns[:, ch * 2:ch * 2 + 2], st[:, 10:12])
                    for k in range(2):
                        tt = ch * 2 + k
                        if tt >= 2:
                            c_tp.wait(v, (tt - 1) * c_tp_step[0])
                        c_ap.inc(v.tensor_scalar(out_ring[:, tt % 2, :],
                                                 src_tile(tt),
                                                 st[:, 8 + k:9 + k],
                                                 st[:, 10 + k:11 + k],
                                                 ALU.mult, ALU.add))

            blk.scalar(_scal)
            blk.vector(_vec)

        c_tp_step = [16]

        for half in range(2):
            c_ap = fresh()
            c_tp = fresh()
            c_sq = fresh()
            c_sr = fresh()
            c_vx = fresh()
            with nc.Block() as blk:
                ld0 = c_ld.n

                def _gpx(g, half=half, ld0=ld0):
                    for ch in range(8):
                        if ch >= 2:
                            c_ap.wait(g, (ch - 1) * 2)
                        c_ld.dinc(g.dma_start(
                            x0st[:, ch % 2, :, :],
                            bass.AP(x0_d, (half * 2048 + ch * 256) * D,
                                    [[D, 128], [128 * D, 2], [1, D]])))

                def _sync(sync, half=half):
                    for tt in range(16):
                        c_ap.wait(sync, tt + 1)
                        c_tp.dinc(sync.dma_start_transpose(
                            bass.AP(lnT_h, tt * 128,
                                    [[8 * 2048, 128], [2048, 8], [1, 128]]),
                            lnring[:, tt % 2, :]))

                blk.gpsimd(_gpx)
                blk.sync(_sync)
                c_tp_step[0] = 16
                emit_ln(blk,
                        lambda i: x0st[:, (i // 2) % 2, i % 2, :],
                        16, None, ld0, 16, lnring, stt_, sqdump,
                        c_ap, c_tp, c_sq, c_sr, c_vx, ld_ctr=c_ld)

            c_mm = fresh()
            c_ev = fresh()
            c_sp = fresh()
            with nc.Block() as blk:
                with nc.psum_tensor(un("ps3a"), [128, 2, 1024], F32) as ps3a:
                    def _t(t):
                        c_cw.wait(t)
                        c_tp.wait(t)
                        for k in range(16):
                            pp = ps3a[:, k % 2, :]
                            if k >= 2:
                                c_ev.wait(t, 2 * (k - 1))
                            for dc in range(8):
                                t.matmul(pp[:, 0:512],
                                         lnT_h[:, dc, k * 128:(k + 1) * 128],
                                         wbig_b[:, dc, 1024:1536],
                                         start=(dc == 0), stop=(dc == 7))
                            for dc in range(8):
                                mm = t.matmul(pp[:, 512:1024],
                                              lnT_h[:, dc, k * 128:(k + 1) * 128],
                                              wbig_b[:, dc, 1536:2048],
                                              start=(dc == 0), stop=(dc == 7))
                            c_mm.inc(mm)

                    def _v(v, half=half):
                        for k in range(16):
                            pp = ps3a[:, k % 2, :]
                            c_mm.wait(v, k + 1)
                            if k >= 2:
                                c_sp.wait(v, 16 * (k - 1))
                            v.tensor_copy(v_sb[:, half * 16 + k, :], pp[:, 0:512])
                            c_ev.inc(v.tensor_copy(p1ring[:, k % 2, 256:512],
                                                   pp[:, 768:1024]))

                    def _sc(sc):
                        for k in range(16):
                            pp = ps3a[:, k % 2, :]
                            c_mm.wait(sc, k + 1)
                            if k >= 2:
                                c_sp.wait(sc, 16 * (k - 1))
                            c_ev.inc(sc.mul(p1ring[:, k % 2, 0:256],
                                            pp[:, 512:768], 1.0 / 16.0))

                    def _gp(g, half=half):
                        for k in range(16):
                            c_ev.wait(g, 2 * (k + 1))
                            c_sp.dinc(g.dma_start(
                                bass.AP(p1_hbm, (half * 16 + k) * 512,
                                        [[32 * 512, 128], [1, 512]]),
                                p1ring[:, k % 2, :]))

                    blk.tensor(_t)
                    blk.vector(_v)
                    blk.scalar(_sc)
                    blk.gpsimd(_gp)

            c_mm = fresh()
            c_ev = fresh()
            with nc.Block() as blk:
                with nc.psum_tensor(un("ps3b"), [128, 2, 512], F32) as ps3b:
                    def _t(t):
                        for i in range(32):
                            ft, sc4 = divmod(i, 4)
                            pp = ps3b[:, i % 2, :]
                            if i >= 2:
                                c_ev.wait(t, i - 1)
                            for dc in range(8):
                                mm = t.matmul(
                                    pp, wbig_b[:, dc, ft * 128:(ft + 1) * 128],
                                    lnT_h[:, dc, sc4 * 512:(sc4 + 1) * 512],
                                    start=(dc == 0), stop=(dc == 7))
                            c_mm.inc(mm)

                    def _sc(sc, half=half):
                        for i in range(32):
                            ft, sc4 = divmod(i, 4)
                            if ft >= 4:
                                continue
                            c_mm.wait(sc, i + 1)
                            c_ev.inc(sc.mul(
                                qT[:, ft, half * 2048 + sc4 * 512:
                                   half * 2048 + (sc4 + 1) * 512],
                                ps3b[:, i % 2, :], 0.125))

                    def _v(v, half=half):
                        for i in range(32):
                            ft, sc4 = divmod(i, 4)
                            if ft < 4:
                                continue
                            c_mm.wait(v, i + 1)
                            c_ev.inc(v.tensor_copy(
                                kT_sb[:, ft - 4, half * 2048 + sc4 * 512:
                                      half * 2048 + (sc4 + 1) * 512],
                                ps3b[:, i % 2, :]))

                    blk.tensor(_t)
                    blk.scalar(_sc)
                    blk.vector(_v)

        s2.close()

        emit_taps([("t_qkT", lambda: qT[:, 0, 0:1024]),
                   ("t_kT", lambda: kT_sb[:, 0, 0:1024]),
                   ("t_v", lambda: v_sb[:, 0, :])])
        sc_wb.close()
        maybe_cut(2)

        with nc.Block() as blk:
            def _v(v):
                with nc.allow_low_precision("landmark mean: f32 internal, bf16 out"):
                    for ft in range(4):
                        v.tensor_reduce(
                            qklT[:, ft, :],
                            bass.AP(qT, ft * S, [[4 * S, 128], [16, 256], [1, 16]]),
                            AX.X, ALU.add)
                        v.tensor_reduce(
                            qklT[:, 4 + ft, :],
                            bass.AP(kT_sb, ft * S, [[4 * S, 128], [16, 256], [1, 16]]),
                            AX.X, ALU.add)
                w = c_sx.inc(v.tensor_copy(stub1[:], stub1[:]))
                c_sx.wait(v, w)
                v.tensor_scalar_mul(
                    bass.AP(qklT, 0, [[8 * LM, 128], [1, 8 * LM]]),
                    bass.AP(qklT, 0, [[8 * LM, 128], [1, 8 * LM]]), 1.0 / 16.0)

            blk.vector(_v)
        emit_taps([("t_ql", lambda: qklT[:, 0:4, :])])
        maybe_cut(3)

        sc_z = ExitStack()
        Z_b = sc_z.enter_context(nc.sbuf_tensor("Z_b", [128, 8, 512], BF16))
        ZT_b = sc_z.enter_context(nc.sbuf_tensor("ZT_b", [128, 8, 512], BF16))
        pv = ExitStack()
        c715 = pv.enter_context(nc.sbuf_tensor("c715_s", [128, 3 * 512], F32))
        c_c715 = fresh()
        c_cb = fresh()
        with nc.Block() as blk:
            def _sy715(sync):
                c_c715.dinc(sync.dma_start(c715[:], c715_d[:]))
            def _gcb(g):
                c_cb.dinc(g.dma_start(
                    bass.AP(cb_b, 0, [[24 * 128, 128], [128, 24], [1, 128]]),
                    bass.AP(cb_d, 0, [[128, 128], [128 * 128, 24], [1, 128]])))
            blk.sync(_sy715)
            blk.gpsimd(_gcb)
        X_b = pv.enter_context(nc.sbuf_tensor("X_b", [128, 8, 512], BF16))
        XT_b = pv.enter_context(nc.sbuf_tensor("XT_b", [128, 8, 512], BF16))
        XZT_b = pv.enter_context(nc.sbuf_tensor("XZT_b", [128, 8, 512], BF16))
        A1_b = pv.enter_context(nc.sbuf_tensor("A1_b", [128, 4, 512], BF16))
        A3_b = pv.enter_context(nc.sbuf_tensor("A3_b", [128, 4, 512], BF16))
        A5_b = pv.enter_context(nc.sbuf_tensor("A5_b", [128, 4, 512], BF16))
        a2tmp = pv.enter_context(nc.sbuf_tensor("a2tmp", [128, 2, 512], F32))
        rowsums = pv.enter_context(nc.sbuf_tensor("rowsums", [128, 16], F32))
        rsc = pv.enter_context(nc.sbuf_tensor("rsc", [128, 16], F32))
        rsi = pv.enter_context(nc.sbuf_tensor("rsi", [128, 16], F32))
        sc4s = pv.enter_context(nc.sbuf_tensor("sc4s", [128, 16], F32))
        iscale = pv.enter_context(nc.sbuf_tensor("iscale", [128, 1], F32))
        c_mm = fresh()
        c_sc = fresh()
        c_a2v = fresh()
        with nc.Block() as blk:
            with nc.psum_tensor("ps4", [128, 2, 512], F32) as ps4:
                def _t(t):
                    for h in range(8):
                        hp, pb = h // 2, (h % 2) * 64
                        pp = ps4[:, h % 2, :]
                        if h >= 2:
                            c_sc.wait(t, h - 1)
                        for lt in range(2):
                            mm = t.matmul(
                                pp[:, lt * 256:(lt + 1) * 256],
                                qklT[pb:pb + 64, hp, lt * 128:(lt + 1) * 128],
                                qklT[pb:pb + 64, 4 + hp, :],
                                start=True, stop=True, tile_position=(pb, 0))
                        c_mm.inc(mm)

                def _sc(sc):
                    for h in range(8):
                        pp = ps4[:, h % 2, :]
                        c_mm.wait(sc, h + 1)
                        if h >= 2:
                            c_a2v.wait(sc, h - 1)
                        sc.activation(a2tmp[:, h % 2, 0:256], pp[:, 0:256],
                                      AF.Exp, accum_out=rsc[:, 2 * h:2 * h + 1])
                        c_sc.inc(sc.activation(
                            a2tmp[:, h % 2, 256:512], pp[:, 256:512], AF.Exp,
                            accum_out=rsc[:, 2 * h + 1:2 * h + 2]))

                def _v(v):
                    for h in range(8):
                        c_sc.wait(v, h + 1)
                        w = c_sx.inc(v.reciprocal(rsi[:, 2 * h:2 * h + 2],
                                                  rsc[:, 2 * h:2 * h + 2]))
                        c_sx.wait(v, w)
                        for lt in range(2):
                            v.tensor_scalar_mul(
                                a2tmp[:, h % 2, lt * 256:(lt + 1) * 256],
                                a2tmp[:, h % 2, lt * 256:(lt + 1) * 256],
                                rsi[:, 2 * h + lt:2 * h + lt + 1])
                        w = c_sx.inc(v.tensor_copy(X_b[:, h, :],
                                                   a2tmp[:, h % 2, :]))
                        c_sx.wait(v, w)
                        for lt in range(2):
                            v.tensor_reduce(rowsums[:, 2 * h + lt:2 * h + lt + 1],
                                            a2tmp[:, h % 2, lt * 256:(lt + 1) * 256],
                                            AX.X, ALU.add)
                        c_a2v.inc(v.tensor_copy(rsi[:, 2 * h:2 * h + 1],
                                                rsi[:, 2 * h:2 * h + 1]))

                blk.tensor(_t)
                blk.scalar(_sc)
                blk.vector(_v)

        maybe_cut(3.2)
        c_mm = fresh()
        c_sc = fresh()
        with nc.Block() as blk:
            with nc.psum_tensor("ps4c", [128, 16], F32) as ps4c:
                def _t(t):
                    for h in range(8):
                        for lt in range(2):
                            for kc in range(2):
                                mm = t.matmul(
                                    ps4c[:, 2 * h + lt:2 * h + lt + 1],
                                    X_b[:, h, kc * 256 + lt * 128:
                                        kc * 256 + (lt + 1) * 128],
                                    onescol[:, :],
                                    start=(kc == 0), stop=(kc == 1))
                        c_mm.inc(mm)

                def _vcm(v):
                    c_mm.wait(v, 8)
                    w = c_sx.inc(v.tensor_reduce(sc4s[:, 1:2], ps4c[:, :],
                                                 AX.X, ALU.max))
                    c_sx.wait(v, w)

                blk.tensor(_t)
                blk.vector(_vcm)

        maybe_cut(3.4)
        c_mm = fresh()
        with nc.Block() as blk:
            with nc.psum_tensor("ps4s", [128, 256], F32) as ps4s:
                def _v1(v):
                    w = c_sx.inc(v.tensor_reduce(sc4s[:, 0:1], rowsums[:],
                                                 AX.X, ALU.max))
                    c_sx.wait(v, w)

                def _t(t):
                    c_sx.wait(t, c_sx.n)
                    t.transpose(ps4s[0:1, 0:128], sc4s[:, 0:1], identf[:])
                    c_mm.inc(t.transpose(ps4s[0:1, 128:256], sc4s[:, 1:2],
                                         identf[:]))

                def _v2(v):
                    c_mm.wait(v, c_mm.n)
                    w = c_sx.inc(v.tensor_reduce(sc4s[0:1, 2:3],
                                                 ps4s[0:1, 0:128], AX.X, ALU.max))
                    c_sx.wait(v, w)
                    w = c_sx.inc(v.tensor_reduce(sc4s[0:1, 3:4],
                                                 ps4s[0:1, 128:256],
                                                 AX.X, ALU.max))
                    c_sx.wait(v, w)
                    w = c_sx.inc(v.tensor_mul(sc4s[0:1, 4:5], sc4s[0:1, 2:3],
                                              sc4s[0:1, 3:4]))
                    c_sx.wait(v, w)

                blk.vector(_v1)
                blk.tensor(_t)
                blk.vector(_v2)

        maybe_cut(3.6)
        c_mm = fresh()
        with nc.Block() as blk:
            with nc.psum_tensor("ps4t", [128, 64], F32) as ps4t:
                def _t(t):
                    c_sx.wait(t, c_sx.n)
                    c_mm.inc(t.matmul(ps4t[:, 0:1], onesf[0:1, :], sc4s[0:1, 4:5],
                                      start=True, stop=True))

                def _v(v):
                    c_mm.wait(v, c_mm.n)
                    w = c_sx.inc(v.reciprocal(iscale[:], ps4t[:, 0:1]))
                    c_sx.wait(v, w)

                blk.tensor(_t)
                blk.vector(_v)

        maybe_cut(3.8)
        c_mm = fresh()
        c_ev = fresh()
        with nc.Block() as blk:
            with nc.psum_tensor("ps4x", [128, 2, 1024], BF16) as ps4x:
                def _t(t):
                    k = 0
                    for h in range(8):
                        for lt_src in range(2):
                            for cb in range(2):
                                pp = ps4x[:, k % 2, cb * 128:(cb + 1) * 128]
                                if k >= 2:
                                    c_ev.wait(t, k - 1)
                                c_mm.inc(t.transpose(
                                    pp, X_b[:, h, lt_src * 256 + cb * 128:
                                            lt_src * 256 + (cb + 1) * 128],
                                    identb[:]))
                                k += 1

                def _v(v):
                    k = 0
                    for h in range(8):
                        for lt_src in range(2):
                            for cb in range(2):
                                pp = ps4x[:, k % 2, cb * 128:(cb + 1) * 128]
                                c_mm.wait(v, k + 1)
                                c_ev.inc(v.tensor_copy(
                                    XT_b[:, h, cb * 256 + lt_src * 128:
                                         cb * 256 + (lt_src + 1) * 128], pp))
                                k += 1
                    for h in range(8):
                        v.tensor_scalar_mul(Z_b[:, h, :], XT_b[:, h, :],
                                            iscale[:, 0:1])
                        v.tensor_scalar_mul(ZT_b[:, h, :], X_b[:, h, :],
                                            iscale[:, 0:1])
                    c_sx.inc(v.tensor_copy(stub1[:], stub1[:]))

                blk.tensor(_t)
                blk.vector(_v)

        emit_taps([("t_att2", lambda: X_b[:, 0:2, :]),
                   ("t_scale", lambda: rsc[:, :])])
        maybe_cut(4)

        c_mm = fresh()
        c_ev = fresh()
        with nc.Block() as blk:
            with nc.psum_tensor("ps5", [128, 8, 512], F32) as ps5:
                for grp in range(2):
                    heads = [grp * 4 + i for i in range(4)]
                    mm0, ev0 = c_mm.n, c_ev.n
                    units = [(it, st_i, hi) for it in range(PITER)
                             for st_i in range(6) for hi in range(4)]
                    uidx = {u: i + 1 for i, u in enumerate(units)}

                    def _t(t, heads=heads, mm0=mm0, ev0=ev0, uidx=uidx,
                           units=units):
                        c_sx.wait(t, c_sx.n)
                        c_c715.wait(t, 16)
                        for (it, st_i, hi) in units:
                            h = heads[hi]
                            pa = ps5[:, hi * 2, :]
                            pb = ps5[:, hi * 2 + 1, :]
                            wv = 0
                            if st_i == 0 and it > 0:
                                wv = uidx[(it - 1, 4, hi)]
                            elif st_i == 1 and it > 0:
                                wv = uidx[(it - 1, 5, hi)]
                            elif st_i >= 2:
                                wv = uidx[(it, st_i - 1, hi)]
                            if wv:
                                c_ev.wait(t, ev0 + wv)
                            if st_i == 0:
                                lh, lj, rh, rj, out = XT_b, h, Z_b, h, pa
                            elif st_i == 1:
                                lh, lj, rh, rj, out = Z_b, h, XT_b, h, pb
                            elif st_i == 2:
                                lh, lj, rh, rj, out = XZT_b, h, A1_b, hi, pa
                            elif st_i == 3:
                                lh, lj, rh, rj, out = XZT_b, h, A3_b, hi, pa
                            elif st_i == 4:
                                lh, lj, rh, rj, out = ZT_b, h, A5_b, hi, pa
                            else:
                                lh, lj, rh, rj, out = A5_b, hi, ZT_b, h, pb
                            for lt in range(2):
                                for kc in range(2):
                                    mm = t.matmul(
                                        out[:, lt * 256:(lt + 1) * 256],
                                        lh[:, lj, kc * 256 + lt * 128:
                                           kc * 256 + (lt + 1) * 128],
                                        rh[:, rj, kc * 256:(kc + 1) * 256],
                                        start=(kc == 0), stop=(kc == 1))
                            c_mm.inc(mm)

                    def _v(v, heads=heads, mm0=mm0, units=units):
                        for i, (it, st_i, hi) in enumerate(units):
                            h = heads[hi]
                            pa = ps5[:, hi * 2, :]
                            pb = ps5[:, hi * 2 + 1, :]
                            c_mm.wait(v, mm0 + i + 1)
                            if st_i == 0:
                                op = v.scalar_tensor_tensor(
                                    A1_b[:, hi, :], pa, -1.0, c715[:, 0:512],
                                    ALU.mult, ALU.add)
                            elif st_i == 1:
                                op = v.tensor_copy(XZT_b[:, h, :], pb)
                            elif st_i == 2:
                                op = v.scalar_tensor_tensor(
                                    A3_b[:, hi, :], pa, -1.0, c715[:, 512:1024],
                                    ALU.mult, ALU.add)
                            elif st_i == 3:
                                op = v.scalar_tensor_tensor(
                                    A5_b[:, hi, :], pa, -1.0, c715[:, 1024:1536],
                                    ALU.mult, ALU.add)
                            elif st_i == 4:
                                op = v.tensor_scalar_mul(Z_b[:, h, :], pa, 0.25)
                            else:
                                op = v.tensor_scalar_mul(ZT_b[:, h, :], pb, 0.25)
                            c_ev.inc(op)

                    blk.tensor(_t)
                    blk.vector(_v)

        emit_taps([("t_ZT", lambda: ZT_b[:, 0:2, :])])
        pv.close()
        maybe_cut(5)


        e3_stack = ExitStack()
        e3 = e3_stack.enter_context(nc.sbuf_tensor("e3", [128, 32, 256], BF16))
        for h in range(8):
            hp, pb = h // 2, (h % 2) * 64
            c_mm = fresh()
            c_sc = fresh()
            c_ev = fresh()
            with nc.Block() as blk:
                with nc.psum_tensor(un("ps6r"), [128, 2, 1024], F32) as ps6r, \
                     nc.psum_tensor(un("ps6c"), [128, 128], F32) as ps6c, \
                     nc.psum_tensor(un("ps6d"), [128, 64], F32) as ps6d, \
                     nc.psum_tensor(un("ps6b"), [128, 128], F32) as ps6b:

                    def _t(t, h=h, hp=hp, pb=pb):
                        for rg in range(8):
                            pp = ps6r[:, rg % 2, :]
                            if rg >= 2:
                                c_sc.wait(t, rg - 1)
                            for j in range(4):
                                tt = rg * 4 + j
                                c_mm.inc(t.matmul(
                                    pp[:, j * 256:(j + 1) * 256],
                                    kT_sb[pb:pb + 64, hp, tt * 128:(tt + 1) * 128],
                                    qklT[pb:pb + 64, hp, :],
                                    start=True, stop=True,
                                    tile_position=(pb, 0)))
                        c_sc.wait(t, 8)
                        for lt in range(2):
                            for cc in range(32):
                                mm = t.matmul(
                                    ps6c[:, lt * 64:(lt + 1) * 64],
                                    e3[:, cc, lt * 128:(lt + 1) * 128],
                                    v_sb[:, cc, h * 64:(h + 1) * 64],
                                    start=(cc == 0), stop=(cc == 31))
                            c_mm.inc(mm)
                        for lt in range(2):
                            for cc in range(32):
                                mm = t.matmul(ps6d[:, lt:lt + 1],
                                              e3[:, cc, lt * 128:(lt + 1) * 128],
                                              onescol[:, :],
                                              start=(cc == 0), stop=(cc == 31))
                            c_mm.inc(mm)
                        c_ev.wait(t, 1)
                        for lt in range(2):
                            for kc in range(2):
                                mm = t.matmul(
                                    ps6b[:, lt * 64:(lt + 1) * 64],
                                    ZT_b[:, h, kc * 256 + lt * 128:
                                         kc * 256 + (lt + 1) * 128],
                                    C_b[:, h, kc * 64:(kc + 1) * 64],
                                    start=(kc == 0), stop=(kc == 1))
                        c_mm.inc(mm)

                    def _sc(sc):
                        for rg in range(8):
                            pp = ps6r[:, rg % 2, :]
                            c_mm.wait(sc, (rg + 1) * 4)
                            c_sc.inc(sc.activation(
                                bass.AP(e3, rg * 4 * 256,
                                        [[32 * 256, 128], [1, 1024]]),
                                pp, AF.Exp))
                    def _v(v, h=h):
                        c_mm.wait(v, 36)
                        w = c_sx.inc(v.reciprocal(d3col[:, :], ps6d[:, 0:2]))
                        c_sx.wait(v, w)
                        c_mm.wait(v, 34)
                        for lt in range(2):
                            op = v.tensor_scalar_mul(
                                C_b[:, h, lt * 64:(lt + 1) * 64],
                                ps6c[:, lt * 64:(lt + 1) * 64],
                                d3col[:, lt:lt + 1])
                        c_ev.inc(op)
                        c_mm.wait(v, 37)
                        c_ev.inc(v.tensor_copy(B_b[:, h, :], ps6b[:, :]))

                    blk.tensor(_t)
                    blk.scalar(_sc)
                    blk.vector(_v)

            if h == 0:
                emit_taps([("t_e3", lambda: e3[:, 0:4, :])])
        e3_stack.close()

        emit_taps([("t_C", lambda: C_b[:, 0:2, :]),
                   ("t_B", lambda: B_b[:, 0:2, :])])
        maybe_cut(6)
        sc_z.close()
        sc_k.close()

        sc_c = ExitStack()
        comb = sc_c.enter_context(nc.sbuf_tensor("comb", [128, 4, S], BF16,
                                                 side="right"))
        dinv7 = sc_c.enter_context(nc.sbuf_tensor("dinv7", [128, 512], F32,
                                                  side="right"))
        t7 = sc_c.enter_context(nc.sbuf_tensor("t7", [128, 512], F32,
                                               side="right"))

        e1_stack = ExitStack()
        e1 = e1_stack.enter_context(nc.sbuf_tensor("e1", [128, 2, 2, 4096], BF16))
        for pp_i in range(4):
            hp = pp_i
            c_mm = fresh()
            c_sc = fresh()
            c_ev = fresh()
            with nc.Block() as blk:
                with nc.psum_tensor(un("ps7r"), [128, 2, 1024], F32) as ps7r, \
                     nc.psum_tensor(un("ps7d"), [128, 512], F32) as ps7d, \
                     nc.psum_tensor(un("ps7u"), [128, 2, 512], F32) as ps7u, \
                     nc.psum_tensor(un("ps7c"), [128, 512], F32) as ps7c:

                    def _t(t, pp_i=pp_i, hp=hp):
                        c_cb.wait(t, 16)
                        for hh in range(2):
                            pb = hh * 64
                            for rg in range(8):
                                lt, scp = divmod(rg, 4)
                                gi = hh * 8 + rg
                                ppp = ps7r[:, gi % 2, :]
                                if gi >= 2:
                                    c_sc.wait(t, gi - 1)
                                for j in range(2):
                                    sc_ = scp * 2 + j
                                    c_mm.inc(t.matmul(
                                        ppp[:, j * 512:(j + 1) * 512],
                                        qklT[pb:pb + 64, 4 + hp,
                                             lt * 128:(lt + 1) * 128],
                                        qT[pb:pb + 64, hp,
                                           sc_ * 512:(sc_ + 1) * 512],
                                        start=True, stop=True,
                                        tile_position=(pb, 0)))
                        c_sc.wait(t, 16)
                        for sc_ in range(8):
                            if sc_ >= 1:
                                c_ev.wait(t, sc_)
                            first = True
                            for hh in range(2):
                                for lt in range(2):
                                    mm = t.matmul(
                                        ps7d[:, :],
                                        m0m1[:, hh * 128:(hh + 1) * 128],
                                        e1[:, hh, lt, sc_ * 512:(sc_ + 1) * 512],
                                        start=first,
                                        stop=(hh == 1 and lt == 1))
                                    first = False
                            c_mm.inc(mm)
                            for hh in range(2):
                                for lt in range(2):
                                    mm = t.matmul(
                                        ps7u[hh * 64:(hh + 1) * 64, sc_ % 2, :],
                                        B_b[:, 2 * pp_i + hh,
                                            lt * 64:(lt + 1) * 64],
                                        e1[:, hh, lt, sc_ * 512:(sc_ + 1) * 512],
                                        start=(lt == 0), stop=(lt == 1),
                                        tile_position=(0, hh * 64))
                            c_mm.inc(mm)
                            for hh in range(2):
                                h = 2 * pp_i + hh
                                for j2 in range(4):
                                    jt = sc_ * 4 + j2
                                    ds = [d for d in range(3)
                                          if 0 <= jt + d - 1 < 32]
                                    for di, d in enumerate(ds):
                                        mm = t.matmul(
                                            ps7c[hh * 64:(hh + 1) * 64,
                                                 j2 * 128:(j2 + 1) * 128],
                                            v_sb[:, jt + d - 1,
                                                 h * 64:(h + 1) * 64],
                                            cb_b[:, h * 3 + d, :],
                                            start=(di == 0),
                                            stop=(di == len(ds) - 1),
                                            tile_position=(0, hh * 64))
                            c_mm.inc(mm)

                    def _sc(sc):
                        for hh in range(2):
                            for rg in range(8):
                                lt, scp = divmod(rg, 4)
                                gi = hh * 8 + rg
                                ppp = ps7r[:, gi % 2, :]
                                c_mm.wait(sc, (gi + 1) * 2)
                                c_sc.inc(sc.activation(
                                    e1[:, hh, lt, scp * 1024:(scp + 1) * 1024],
                                    ppp, AF.Exp))

                    def _v(v, pp_i=pp_i):
                        for sc_ in range(8):
                            base = 32 + sc_ * 3
                            c_mm.wait(v, base + 1)
                            w = c_sx.inc(v.reciprocal(dinv7[:, :], ps7d[:, :]))
                            c_sx.wait(v, w)
                            c_mm.wait(v, base + 2)
                            v.tensor_mul(t7[:, :], ps7u[:, sc_ % 2, :],
                                         dinv7[:, :])
                            c_mm.wait(v, base + 3)
                            c_ev.inc(v.tensor_add(
                                comb[:, pp_i, sc_ * 512:(sc_ + 1) * 512],
                                t7[:, :], ps7c[:, :]))

                    blk.tensor(_t)
                    blk.scalar(_sc)
                    blk.vector(_v)

            if pp_i == 0:
                emit_taps([("t_e1", lambda: e1[:, 0, 0, 0:1024])])
        e1_stack.close()
        sc_a.close()

        emit_taps([("t_comb", lambda: comb[:, 0, 0:1024])])
        maybe_cut(7)

        sc_x = ExitStack()
        outw_b = sc_x.enter_context(nc.sbuf_tensor("outw_s", [128, 4, 1024], BF16))
        xg0T = sc_x.enter_context(nc.sbuf_tensor("xg0T", [128, 8, S], BF16))

        c_mm = fresh()
        c_ev = fresh()
        c_ow = fresh()
        with nc.Block() as blk:
            with nc.psum_tensor("ps8", [128, 2, 512], F32) as ps8:
                def _gp(g):
                    c_ow.dinc(g.dma_start(
                        bass.AP(outw_b, 0, [[4 * 1024, 128], [1024, 4], [1, 1024]]),
                        bass.AP(outw_d, 0,
                                [[1024, 128], [128 * 1024, 4], [1, 1024]])))

                def _t(t):
                    c_ow.wait(t, 16)
                    for i in range(64):
                        dmo, sc_ = divmod(i, 8)
                        pp = ps8[:, i % 2, :]
                        if i >= 2:
                            c_ev.wait(t, i - 1)
                        for ic in range(4):
                            mm = t.matmul(pp,
                                          outw_b[:, ic, dmo * 128:(dmo + 1) * 128],
                                          comb[:, ic, sc_ * 512:(sc_ + 1) * 512],
                                          start=(ic == 0), stop=(ic == 3))
                        c_mm.inc(mm)

                def _v(v):
                    for i in range(64):
                        dmo, sc_ = divmod(i, 8)
                        pp = ps8[:, i % 2, :]
                        c_mm.wait(v, i + 1)
                        c_ev.inc(v.tensor_scalar_add(
                            xg0T[:, dmo, sc_ * 512:(sc_ + 1) * 512], pp,
                            outb_t[:, dmo:dmo + 1]))

                blk.gpsimd(_gp)
                blk.tensor(_t)
                blk.vector(_v)

        sc_c.close()
        emit_taps([("t_xg0T", lambda: xg0T[:, 0, 0:1024])])
        maybe_cut(8)

        sc_q = ExitStack()
        q0k0 = sc_q.enter_context(nc.sbuf_tensor("q0k0", [128, 32, 512], BF16,
                                                 side="right"))
        p1r = sc_q.enter_context(nc.sbuf_tensor("p1r", [128, 2, 512], BF16,
                                                side="right"))
        wbig2 = sc_q.enter_context(nc.sbuf_tensor("wbig2", [128, 8, 512], BF16,
                                                  side="right"))
        c_wb2 = fresh()
        sc_xn = ExitStack()
        xg0NF = sc_xn.enter_context(nc.sbuf_tensor("xg0NF", [128, 32, 1024], BF16))

        c_mm = fresh()
        c_ev = fresh()
        c_pl = fresh()
        with nc.Block() as blk:
            with nc.psum_tensor("ps9", [128, 2, 512], F32) as ps9:
                def _gp(g):
                    c_wb2.dinc(g.dma_start(
                        bass.AP(wbig2, 0, [[8 * 512, 128], [512, 8], [1, 512]]),
                        bass.AP(wbig_d, 1536,
                                [[2048, 128], [128 * 2048, 8], [1, 512]])))
                    for tt in range(32):
                        if tt >= 2:
                            c_ev.wait(g, tt - 1)
                        c_pl.dinc(g.dma_start(
                            p1r[:, tt % 2, :],
                            bass.AP(p1_hbm, tt * 512, [[32 * 512, 128], [1, 512]])))

                def _t(t):
                    c_wb2.wait(t, 16)
                    for tt in range(32):
                        pp = ps9[:, tt % 2, :]
                        if tt >= 2:
                            c_ev.wait(t, tt - 1)
                        for dc in range(8):
                            mm = t.matmul(pp, xg0T[:, dc, tt * 128:(tt + 1) * 128],
                                          wbig2[:, dc, :],
                                          start=(dc == 0), stop=(dc == 7))
                        c_mm.inc(mm)

                def _v(v):
                    for tt in range(32):
                        pp = ps9[:, tt % 2, :]
                        c_mm.wait(v, tt + 1)
                        c_pl.wait(v, (tt + 1) * 16)
                        v.scalar_tensor_tensor(q0k0[:, tt, 0:256], pp[:, 0:256],
                                               1.0 / 16.0, p1r[:, tt % 2, 0:256],
                                               ALU.mult, ALU.add)
                        c_ev.inc(v.tensor_add(q0k0[:, tt, 256:512],
                                              pp[:, 256:512],
                                              p1r[:, tt % 2, 256:512]))

                blk.gpsimd(_gp)
                blk.tensor(_t)
                blk.vector(_v)

        c_tp9 = fresh()
        c_sel9 = fresh()
        with nc.Block() as blk:
            def _sync(sync):
                for dc in range(8):
                    c_tp9.dinc(sync.dma_start_transpose(
                        bass.AP(xg0NF, dc * 128,
                                [[32 * 1024, 128], [1024, 32], [1, 128]]),
                        xg0T[:, dc, :]))

            def _v(v):
                c_tp9.wait(v)
                for tt in range(NT):
                    v.tensor_scalar_mul(xg0NF[:, tt, :], xg0NF[:, tt, :],
                                        mks_s[:, 0:1])
                for tt in range(NT):
                    op = v.scalar_tensor_tensor(
                        xg0NF[:, tt, :], xg0NF[:, NT + tt, :], mks_s[:, 1:2],
                        xg0NF[:, tt, :], ALU.mult, ALU.add)
                c_sel9.inc(op)

            def _sy2(sync):
                c_sel9.wait(sync)
                c_tp9.dinc(sync.dma_start(
                    xg0n_hbm[:],
                    bass.AP(xg0NF, 0, [[32 * 1024, 128], [1, NT * 1024]])))

            blk.sync(_sync)
            blk.vector(_v)
            blk.sync(_sy2)

        emit_taps([("t_q0k0", lambda: q0k0[:, 0, :])])
        maybe_cut(9)
        sc_xn.close()
        sc_x.close()

        sc_g = ExitStack()
        a2ct_b = sc_g.enter_context(nc.sbuf_tensor("a2ct_b", [128, 32, ndp], BF16))
        pm_b = sc_g.enter_context(nc.sbuf_tensor("pm_b", [128, ndp // 128, S], BF16))
        g2c_b = sc_g.enter_context(
            nc.sbuf_tensor("g2c_b", [128, ndp // 128, 256], BF16))
        atring = sc_g.enter_context(
            nc.sbuf_tensor("atring", [128, 2, 32, 128], BF16))
        g10t = sc_g.enter_context(nc.sbuf_tensor("g10t", [128, 256], F32))
        g2f = sc_g.enter_context(nc.sbuf_tensor("g2f", [128, 256], F32))
        cv_b = sc_g.enter_context(nc.sbuf_tensor("cv_b", [128, 32, mpad], BF16))
        w4095 = sc_g.enter_context(nc.sbuf_tensor("w4095", [128, 32], F32))
        k0row = sc_g.enter_context(nc.sbuf_tensor("k0row", [1, 256], BF16))
        k0bc = sc_g.enter_context(nc.sbuf_tensor("k0bc", [128, 256], BF16))
        ea_s = sc_g.enter_context(nc.sbuf_tensor("ea_s", [128, 32], F32))
        easum = sc_g.enter_context(nc.sbuf_tensor("easum", [128, 4], F32))

        c_a2 = fresh()
        c_mm = fresh()
        c_ev = fresh()
        c_ar = fresh()
        NDT = ndp // 128
        gd0 = c_gd.n
        SX0 = [0]
        with nc.Block() as blk:
            with nc.psum_tensor("psg2c", [128, 2, 512], F32) as psg2, \
                 nc.psum_tensor("psG", [128, 2, 512], F32) as psG:

                def _gp(g):
                    c_a2.dinc(g.dma_start(
                        bass.AP(cv_b, 0, [[32 * mpad, 128], [mpad, 32], [1, mpad]]),
                        bass.AP(cv_d, 0, [[mpad, 128], [128 * mpad, 32], [1, mpad]])))
                    c_a2.dinc(g.dma_start(
                        k0row[0:1, :],
                        bass.AP(q0k0, 127 * 32 * 512 + 31 * 512 + 256,
                                [[32 * 512, 1], [1, 256]])))
                    c_a2.dinc(g.dma_start(
                        bass.AP(a2ct_b, 0, [[32 * ndp, 128], [ndp, 32], [1, ndp]]),
                        bass.AP(a2ct_d, 0, [[ndp, 128], [128 * ndp, 32], [1, ndp]])))
                    c_a2.dinc(g.dma_start(
                        bass.AP(pm_b, 0, [[NDT * S, 128], [S, NDT], [1, S]]),
                        bass.AP(pm_d, 0, [[S, 128], [128 * S, NDT], [1, S]])))
                    for rt in range(32):
                        if rt >= 2:
                            c_mm.wait(g, 1 + NDT + rt - 1)
                        c_gd.dinc(g.dma_start(
                            bass.AP(atring, (rt % 2) * 32 * 128,
                                    [[2 * 32 * 128, 128], [128, 32], [1, 128]]),
                            bass.AP(at_d, rt * 128,
                                    [[S, 128], [128 * S, 32], [1, 128]])))

                def _t(t, gd0=gd0, sx0=None):
                    c_a2.wait(t)
                    c_mm.inc(t.matmul(psG[:, 0, 0:256], onesb[0:1, :], k0row[0:1, :],
                                      start=True, stop=True))
                    for et in range(NDT):
                        for cc in range(32):
                            mm = t.matmul(psg2[:, et % 2, 0:256],
                                          a2ct_b[:, cc, et * 128:(et + 1) * 128],
                                          q0k0[:, cc, 256:512],
                                          start=(cc == 0), stop=(cc == 31))
                        c_mm.inc(mm)
                    c_ev.wait(t, NDT)
                    for rt in range(32):
                        c_gd.wait(t, gd0 + (rt + 1) * 16)
                        pp = psG[:, rt % 2, 0:256]
                        if rt >= 2:
                            c_ev.wait(t, NDT + rt - 1)
                        for cc in range(32):
                            t.matmul(pp, atring[:, rt % 2, cc, :],
                                     q0k0[:, cc, 256:512],
                                     start=(cc == 0), stop=False)
                        for et in range(NDT):
                            mm = t.matmul(pp, pm_b[:, et, rt * 128:(rt + 1) * 128],
                                          g2c_b[:, et, :],
                                          start=False, stop=(et == NDT - 1))
                        c_mm.inc(mm)

                def _v(v):
                    c_mm.wait(v, 1)
                    w = c_sx.inc(v.tensor_copy(k0bc[:, :], psG[:, 0, 0:256]))
                    assert w == SX0[0] + 1
                    v.wait_ge(c_sx.h, w)
                    v.tensor_reduce(
                        w4095[:, :],
                        bass.AP(cv_b, 0, [[32 * mpad, 128], [mpad, 32], [1, mpad]]),
                        AX.X, ALU.add)
                    c_ev.inc(v.tensor_copy(stub1[:], stub1[:]))
                    for et in range(NDT):
                        c_mm.wait(v, 1 + et + 1)
                        c_ev.inc(v.tensor_copy(g2c_b[:, et, :], psg2[:, et % 2, 0:256]))
                    for rt in range(32):
                        pp = psG[:, rt % 2, 0:256]
                        c_mm.wait(v, 1 + NDT + rt + 1)
                        v.scalar_tensor_tensor(g2f[:, :], k0bc[:, :],
                                               w4095[:, rt:rt + 1], pp,
                                               ALU.mult, ALU.add)
                        v.tensor_mul(g10t[:, :], q0k0[:, rt, 0:256], g2f[:, :])
                        c_ev.inc(v.tensor_reduce(araw[:, rt:rt + 1], g10t[:, :],
                                                 AX.X, ALU.add))
                    c_ar.inc(v.tensor_copy(easum[:, 1:2], eps_t[:]))

                blk.gpsimd(_gp)
                SX0[0] = c_sx.n
                blk.tensor(_t)
                blk.vector(_v)

        c_mm = fresh()
        c_se = fresh()
        c_ao = fresh()
        with nc.Block() as blk:
            with nc.psum_tensor("psal", [128, 64], F32) as psal:
                def _sc(sc):
                    c_ar.wait(sc)
                    c_se.inc(sc.activation(ea_s[:], araw[:], AF.Exp,
                                           accum_out=easum[:, 0:1]))

                def _t(t):
                    c_se.wait(t, 1)
                    c_mm.inc(t.matmul(psal[:, 0:1], onesf[:, :], easum[:, 0:1],
                                      start=True, stop=True))

                def _v(v):
                    c_mm.wait(v, c_mm.n)
                    w = c_sx.inc(v.reciprocal(easum[:, 2:3], psal[:, 0:1]))
                    c_sx.wait(v, w)
                    w = c_sx.inc(v.tensor_scalar_mul(alpha_sb[:], ea_s[:],
                                                     easum[:, 2:3]))
                    c_sx.wait(v, w)
                    c_vsm.inc(v.tensor_copy(stub1[:], stub1[:]))

                def _v2(v):
                    w = c_sx.inc(v.tensor_scalar_mul(alpha_own[:, :],
                                                     alpha_sb[:, 0:16],
                                                     mks_s[:, 0:1]))
                    c_sx.wait(v, w)
                    c_ao.inc(v.scalar_tensor_tensor(
                        alpha_own[:, :], alpha_sb[:, 16:32], mks_s[:, 1:2],
                        alpha_own[:, :], ALU.mult, ALU.add))

                blk.scalar(_sc)
                blk.tensor(_t)
                blk.vector(_v)
                blk.vector(_v2)

        emit_taps([("t_araw", lambda: araw[:, :]),
                   ("t_alpha", lambda: alpha_sb[:, :])])
        maybe_cut(10)
        sc_q.close()
        sc_g.close()

        sc_o = ExitStack()
        xo_f32 = sc_o.enter_context(nc.sbuf_tensor("xo_f32", [128, NT, D], F32,
                                                   side="right"))
        sc_m = ExitStack()
        wvbbc = sc_m.enter_context(nc.sbuf_tensor("wvbbc_s", [128, 1024], F32))
        wvb_row = sc_m.enter_context(nc.sbuf_tensor("wvb_s", [1, 1024], F32))
        c_wvl = fresh()
        wv_b = sc_m.enter_context(nc.sbuf_tensor("wv_s", [128, 8, 1024], BF16))
        lnownT = sc_m.enter_context(nc.sbuf_tensor("lnownT", [128, 8, 2048], BF16))
        xg0ownN = sc_m.enter_context(
            nc.sbuf_tensor("xg0ownN2", [128, NT, 1024], BF16))
        c_xnr = fresh()
        lnring2 = sc_m.enter_context(nc.sbuf_tensor("lnring2", [128, 2, 1024], BF16))
        sq2 = sc_m.enter_context(nc.sbuf_tensor("sq2", [128, 1024], F32))
        st2 = sc_m.enter_context(nc.sbuf_tensor("st2", [128, 12], F32))
        mx_enc = sc_m.enter_context(nc.sbuf_tensor("mx_enc", [128, 1024], F32))
        mx_a = sc_m.enter_context(nc.sbuf_tensor("mx_a", [128, 1024], F32))
        mx_xl = sc_m.enter_context(nc.sbuf_tensor("mx_xl", [128, 2, 1024], F32))
        mx_sg = sc_m.enter_context(nc.sbuf_tensor("mx_sg", [128, 1024], F32))
        mx_sw = sc_m.enter_context(nc.sbuf_tensor("mx_sw", [128, 2, 1024], F32))

        c_ap = fresh()
        c_tp = fresh()
        c_sq = fresh()
        c_sr = fresh()
        c_vx = fresh()
        c_wvb = fresh()
        with nc.Block() as blk:
            def _gp(g):
                c_tp9.wait(g)
                c_xnr.dinc(g.dma_start(xg0ownN[:], xg0n_hbm[:]))
                c_cw.dinc(g.dma_start(
                    bass.AP(wv_b, 0, [[8 * 1024, 128], [1024, 8], [1, 1024]]),
                    bass.AP(wv_d, 0, [[1024, 128], [128 * 1024, 8], [1, 1024]])))

            def _sync(sync):
                c_wvl.dinc(sync.dma_start(wvb_row[:], wvb_d[:]))
                c_xo.dinc(sync.dma_start(
                    bass.AP(xo_f32, 0, [[NT * D, 128], [D, NT], [1, D]]),
                    bass.AP(xo_d, 0, [[D, 128], [128 * D, NT], [1, D]])))
                for tt in range(16):
                    c_ap.wait(sync, tt + 1)
                    c_tp.dinc(sync.dma_start_transpose(
                        bass.AP(lnownT, tt * 128,
                                [[8 * 2048, 128], [2048, 8], [1, 128]]),
                        lnring2[:, tt % 2, :]))

            def _pre_s(sc):
                c_xo.wait(sc)

            def _pre_v(v):
                c_xo.wait(v)

            blk.gpsimd(_gp)
            blk.sync(_sync)
            blk.scalar(_pre_s)
            blk.vector(_pre_v)
            c_tp_step[0] = 16
            emit_ln(blk, lambda i: xo_f32[:, i, :], 16, None, 0, 0,
                    lnring2, st2, sq2, c_ap, c_tp, c_sq, c_sr, c_vx,
                    save_stats=(ro_st, no_st))

        c_mm = fresh()
        c_ev = fresh()
        c_sg = fresh()
        c_xv = fresh()
        with nc.Block() as blk:
            with nc.psum_tensor("ps11", [128, 2, 1024], F32) as ps11, \
                 nc.psum_tensor("pswvb", [128, 1024], F32) as pswvb:
                def _t(t):
                    c_wvl.wait(t, 16)
                    t.matmul(pswvb[:, 0:512], onesf[0:1, :], wvb_row[:, 0:512],
                             start=True, stop=True)
                    c_wvb.inc(t.matmul(pswvb[:, 512:1024], onesf[0:1, :],
                                       wvb_row[:, 512:1024], start=True,
                                       stop=True))
                    c_tp.wait(t)
                    c_cw.wait(t)
                    for tt in range(16):
                        pp = ps11[:, tt % 2, :]
                        if tt >= 2:
                            c_ev.wait(t, tt - 1)
                        for dc in range(8):
                            for jh in range(2):
                                mm = t.matmul(
                                    pp[:, jh * 512:(jh + 1) * 512],
                                    lnownT[:, dc, tt * 128:(tt + 1) * 128],
                                    wv_b[:, dc, jh * 512:(jh + 1) * 512],
                                    start=(dc == 0), stop=(dc == 7))
                        c_mm.inc(mm)

                def _v(v):
                    c_ao.wait(v, c_ao.n)
                    c_xnr.wait(v)
                    c_wvb.wait(v, 1)
                    v.tensor_copy(wvbbc[:], pswvb[:])
                    for tt in range(16):
                        pp = ps11[:, tt % 2, :]
                        c_mm.wait(v, tt + 1)
                        if tt >= 2:
                            c_sg.wait(v, tt - 1)
                        v.tensor_add(mx_a[:], pp, wvbbc[:])
                        c_xv.inc(v.tensor_scalar_mul(mx_xl[:, tt % 2, :],
                                                     mx_a[:],
                                                     alpha_own[:, tt:tt + 1]))
                        v.tensor_scalar(mx_enc[:], xo_f32[:, tt, :],
                                        ro_st[:, tt:tt + 1], no_st[:, tt:tt + 1],
                                        ALU.mult, ALU.add)
                        w2 = c_sx.inc(v.tensor_add(mx_enc[:], mx_enc[:],
                                                   xg0ownN[:, tt, :]))
                        c_sx.wait(v, w2)
                        v.tensor_sub(mx_a[:], mx_xl[:, tt % 2, :], mx_enc[:])
                        c_sg.wait(v, tt + 1)
                        v.tensor_mul(mx_a[:], mx_sw[:, tt % 2, :], mx_a[:])
                        w2 = c_sx.inc(v.tensor_add(mx_a[:], mx_a[:], mx_enc[:]))
                        c_sx.wait(v, w2)
                        c_ev.inc(v.scalar_tensor_tensor(
                            xo_f32[:, tt, :], mx_a[:], 2.0, xo_f32[:, tt, :],
                            ALU.mult, ALU.add))

                def _sc(sc):
                    for tt in range(16):
                        c_xv.wait(sc, tt + 1)
                        if tt >= 2:
                            c_ev.wait(sc, tt - 1)
                        sc.activation(mx_sg[:], mx_xl[:, tt % 2, :], AF.Sigmoid,
                                      scale=negone[:, 0:1])
                        c_sg.inc(sc.activation(mx_sw[:, tt % 2, :], mx_sg[:],
                                               AF.Square))

                blk.tensor(_t)
                blk.vector(_v)
                blk.scalar(_sc)

        emit_taps([("t_x1", lambda: xo_f32[:, 0, :])])
        maybe_cut(11)
        sc_m.close()

        sc_y = ExitStack()
        yT = sc_y.enter_context(nc.sbuf_tensor("yT", [128, 8, TOK], BF16))
        sc_ylo = ExitStack()
        yTlo = sc_ylo.enter_context(nc.sbuf_tensor("yTlo", [128, 8, TOK], BF16))
        yring = sc_ylo.enter_context(nc.sbuf_tensor("yring", [128, 2, 1024], F32))
        yhir = sc_ylo.enter_context(nc.sbuf_tensor("yhir", [128, 2, 1024], BF16))
        ylor = sc_ylo.enter_context(nc.sbuf_tensor("ylor", [128, 2, 1024], BF16))
        sq3 = sc_ylo.enter_context(nc.sbuf_tensor("sq3", [128, 1024], F32))
        st3 = sc_ylo.enter_context(nc.sbuf_tensor("st3", [128, 12], F32))
        elog = sc_ylo.enter_context(nc.sbuf_tensor("elog", [128, NT, 8], F32))
        lcp = sc_ylo.enter_context(nc.sbuf_tensor("lcp", [128, NT, 8], F32))
        gtmp = sc_ylo.enter_context(nc.sbuf_tensor("gtmp", [128, 7, 16], F32))

        c_ap = fresh()
        c_tp12 = fresh()
        c_sq = fresh()
        c_sr = fresh()
        c_vx = fresh()
        with nc.Block() as blk:
            def _sync(sync):
                for tt in range(16):
                    c_ap.wait(sync, tt + 1)
                    c_tp12.dinc(sync.dma_start_transpose(
                        bass.AP(yT, tt * 128, [[8 * TOK, 128], [TOK, 8], [1, 128]]),
                        yhir[:, tt % 2, :]))
                    c_tp12.dinc(sync.dma_start_transpose(
                        bass.AP(yTlo, tt * 128, [[8 * TOK, 128], [TOK, 8], [1, 128]]),
                        ylor[:, tt % 2, :]))

            def _scal(sc):
                for ch in range(8):
                    if ch >= 1:
                        c_vx.wait(sc, ch * 6 - 4)
                    sc.activation(sq3[:], xo_f32[:, ch * 2, :], AF.Square,
                                  accum_out=st3[:, 2:3])
                    c_sq.inc(sc.activation(sq3[:], xo_f32[:, ch * 2 + 1, :],
                                           AF.Square, accum_out=st3[:, 3:4]))
                    c_vx.wait(sc, ch * 6 + 4)
                    c_sr.inc(sc.activation(st3[:, 8:10], st3[:, 4:6],
                                           AF.Sqrt, bias=eps_t[:]))

            def _vec(v):
                for ch in range(8):
                    v.tensor_reduce(st3[:, 0:1], xo_f32[:, ch * 2, :],
                                    AX.X, ALU.add)
                    w = c_vx.inc(v.tensor_reduce(st3[:, 1:2],
                                                 xo_f32[:, ch * 2 + 1, :],
                                                 AX.X, ALU.add))
                    c_vx.wait(v, w)
                    c_sq.wait(v, ch + 1)
                    v.tensor_scalar_mul(st3[:, 0:2], st3[:, 0:2], 1.0 / D)
                    w = c_vx.inc(v.tensor_scalar_mul(st3[:, 4:6], st3[:, 2:4],
                                                     1.0 / D))
                    c_vx.wait(v, w)
                    w = c_vx.inc(v.tensor_mul(st3[:, 6:8], st3[:, 0:2],
                                              st3[:, 0:2]))
                    c_vx.wait(v, w)
                    w = c_vx.inc(v.tensor_sub(st3[:, 4:6], st3[:, 4:6],
                                              st3[:, 6:8]))
                    c_vx.wait(v, w)
                    c_sr.wait(v, ch + 1)
                    w = c_vx.inc(v.reciprocal(st3[:, 8:10], st3[:, 8:10]))
                    c_vx.wait(v, w)
                    w = c_vx.inc(v.scalar_tensor_tensor(st3[:, 10:12],
                                                        st3[:, 0:2], -1.0,
                                                        st3[:, 8:10],
                                                        ALU.mult, ALU.mult))
                    c_vx.wait(v, w)
                    for k in range(2):
                        tt = ch * 2 + k
                        if tt >= 2:
                            c_tp12.wait(v, (tt - 1) * 32)
                        v.tensor_scalar(yring[:, tt % 2, :], xo_f32[:, tt, :],
                                        st3[:, 8 + k:9 + k], st3[:, 10 + k:11 + k],
                                        ALU.mult, ALU.add)
                        w = c_sx.inc(v.tensor_copy(yhir[:, tt % 2, :],
                                                   yring[:, tt % 2, :]))
                        c_sx.wait(v, w)
                        c_ap.inc(v.tensor_sub(ylor[:, tt % 2, :],
                                              yring[:, tt % 2, :],
                                              yhir[:, tt % 2, :]))

            blk.sync(_sync)
            blk.scalar(_scal)
            blk.vector(_vec)

        c_mm = fresh()
        c_lv = fresh()
        with nc.Block() as blk:
            with nc.psum_tensor("ps12", [128, 2, 512], F32) as ps12:
                def _t(t):
                    c_tp12.wait(t)
                    for tt in range(16):
                        pp = ps12[:, tt % 2, 0:8]
                        if tt >= 2:
                            c_lv.wait(t, tt - 1)
                        passes = [(yT, wrt_hi), (yT, wrt_lo), (yTlo, wrt_hi)]
                        for pi, (ya, wa) in enumerate(passes):
                            for dc in range(8):
                                mm = t.matmul(pp,
                                              ya[:, dc, tt * 128:(tt + 1) * 128],
                                              wa[:, dc, :],
                                              start=(pi == 0 and dc == 0),
                                              stop=(pi == 2 and dc == 7))
                        c_mm.inc(mm)

                def _v(v):
                    for tt in range(16):
                        pp = ps12[:, tt % 2, 0:8]
                        c_mm.wait(v, tt + 1)
                        c_lv.inc(v.tensor_copy(lcp[:, tt, :], pp))

                blk.tensor(_t)
                blk.vector(_v)

        c_se = fresh()
        with nc.Block() as blk:
            def _sc(sc):
                c_lv.wait(sc)
                c_se.inc(sc.activation(
                    bass.AP(elog, 0, [[NT * 8, 128], [1, NT * 8]]),
                    bass.AP(lcp, 0, [[NT * 8, 128], [1, NT * 8]]), AF.Exp))

            def _v(v):
                c_se.wait(v)
                E0 = bass.AP(elog, 0, [[NT * 8, 128], [8, NT], [1, 1]])
                EMX, DEN, GM = gtmp[:, 0, :], gtmp[:, 1, :], gtmp[:, 2, :]
                F_, EQ, T2, FIRST = (gtmp[:, 3, :], gtmp[:, 4, :],
                                     gtmp[:, 5, :], gtmp[:, 6, :])
                w = c_sx.inc(v.tensor_reduce(
                    EMX, bass.AP(elog, 1, [[NT * 8, 128], [8, NT], [1, 7]]),
                    AX.X, ALU.max))
                c_sx.wait(v, w)
                w = c_sx.inc(v.tensor_tensor(DEN, E0, EMX, ALU.add))
                c_sx.wait(v, w)
                w = c_sx.inc(v.reciprocal(DEN, DEN))
                c_sx.wait(v, w)
                w = c_sx.inc(v.tensor_tensor(gates_e[:, 0, :], E0, DEN, ALU.mult))
                c_sx.wait(v, w)
                w = c_sx.inc(v.tensor_tensor(GM, EMX, DEN, ALU.mult))
                c_sx.wait(v, w)
                v.memset(F_, 0.0)
                for j in range(1, 8):
                    EJ = bass.AP(elog, j, [[NT * 8, 128], [8, NT], [1, 1]])
                    w = c_sx.inc(v.tensor_tensor(EQ, EJ, EMX, ALU.is_equal))
                    c_sx.wait(v, w)
                    w = c_sx.inc(v.tensor_tensor(T2, EQ, F_, ALU.mult))
                    c_sx.wait(v, w)
                    w = c_sx.inc(v.tensor_sub(FIRST, EQ, T2))
                    c_sx.wait(v, w)
                    w = c_sx.inc(v.tensor_max(F_, F_, FIRST))
                    c_sx.wait(v, w)
                    w = c_sx.inc(v.tensor_tensor(gates_e[:, j, :], FIRST, GM,
                                                 ALU.mult))
                    c_sx.wait(v, w)
                c_vsm.inc(v.tensor_copy(stub1[:], stub1[:]))

            blk.scalar(_sc)
            blk.vector(_v)

        emit_taps([("t_lgt", lambda: bass.AP(lcp, 0, [[NT * 8, 128], [1, 128]])),
                   ("t_y", lambda: bass.AP(yT, 0, [[8 * TOK, 128], [1, 1024]])),
                   ("t_gates", lambda: bass.AP(gates_e, 0,
                                               [[8 * NT, 128], [1, 128]]))])
        sc_ylo.close()
        maybe_cut(12)

        sc_moe = ExitStack()
        hsl = sc_moe.enter_context(nc.sbuf_tensor("hsl", [128, 16, 1024], BF16))
        w1r = sc_moe.enter_context(nc.sbuf_tensor("w1r", [128, 2, 8, 512], BF16))
        w3r = sc_moe.enter_context(nc.sbuf_tensor("w3r", [128, 2, 8, 512], BF16))
        w2buf = sc_moe.enter_context(nc.sbuf_tensor("w2buf", [128, 16, 1024], BF16))
        stmp = sc_moe.enter_context(nc.sbuf_tensor("stmp", [128, 1024], BF16))

        c_w13 = fresh()
        c_w2l = fresh()
        c_mmh = fresh()
        c_mmy = fresh()
        c_sil = fresh()
        c_hev = fresh()
        c_yev = fresh()

        def hid(e, ffh, th, ffc):
            return ((e * 2 + ffh) * 2 + th) * 16 + ffc + 1

        def yid(e, ffh, th, t):
            return ((e * 2 + ffh) * 2 + th) * 8 + t + 1

        with nc.Block() as blk:
            with nc.psum_tensor("psmoe", [128, 4, 1024], F32) as PS:
                def _gp(g):
                    for e in range(NEXP):
                        for ffh in range(2):
                            q = e * 2 + ffh
                            if q >= 1:
                                pe, pffh = divmod(q - 1, 2)
                                c_mmy.wait(g, yid(pe, pffh, 1, 7))
                            c_w2l.dinc(g.dma_start(
                                bass.AP(w2buf, 0, [[16 * 1024, 128],
                                                   [1024, 16], [1, 1024]]),
                                bass.AP(w2t_d, (e * DFF + ffh * 2048) * D,
                                        [[D, 128], [128 * D, 16], [1, D]])))
                            for th in range(2):
                                for c in range(4):
                                    gidx = (q * 2 + th) * 4 + c
                                    if gidx >= 2:
                                        pg = gidx - 2
                                        pqth, pc = divmod(pg, 4)
                                        c_mmh.wait(g, pqth * 16 + pc * 4 + 4)
                                    for (ring, wd) in [(w1r, w1t_d),
                                                       (w3r, w3t_d)]:
                                        c_w13.dinc(g.dma_start(
                                            bass.AP(ring, (gidx % 2) * 8 * 512,
                                                    [[2 * 8 * 512, 128],
                                                     [512, 8], [1, 512]]),
                                            bass.AP(wd, e * D * DFF +
                                                    ffh * 2048 + c * 512,
                                                    [[DFF, 128],
                                                     [128 * DFF, 8], [1, 512]])))

                def _t(t):
                    c_vsm.wait(t)
                    c_tp12.wait(t)
                    for e in range(NEXP):
                        for ffh in range(2):
                            q = e * 2 + ffh
                            for th in range(2):
                                if q * 2 + th > 0:
                                    c_yev.wait(t, c_yev_expect(e, ffh, th))
                                for ffc in range(16):
                                    gidx = (q * 2 + th) * 4 + ffc // 4
                                    c_w13.wait(t, (gidx + 1) * 32)
                                    if ffc >= 2:
                                        c_hev.wait(t, hid(e, ffh, th, ffc - 2))
                                    p1p = PS[:, ffc % 2, :]
                                    p3p = PS[:, 2 + ffc % 2, :]
                                    sl = gidx % 2
                                    co = (ffc % 4) * 128
                                    for dc in range(8):
                                        for jj in range(2):
                                            t.matmul(
                                                p1p[:, jj * 512:(jj + 1) * 512],
                                                w1r[:, sl, dc, co:co + 128],
                                                yT[:, dc,
                                                   th * 1024 + jj * 512:
                                                   th * 1024 + (jj + 1) * 512],
                                                start=(dc == 0), stop=(dc == 7))
                                        for jj in range(2):
                                            mm = t.matmul(
                                                p3p[:, jj * 512:(jj + 1) * 512],
                                                w3r[:, sl, dc, co:co + 128],
                                                yT[:, dc,
                                                   th * 1024 + jj * 512:
                                                   th * 1024 + (jj + 1) * 512],
                                                start=(dc == 0), stop=(dc == 7))
                                    c_mmh.inc(mm)
                                c_w2l.wait(t, (q + 1) * 16)
                                c_hev.wait(t, hid(e, ffh, th, 15))
                                for tu in range(8):
                                    pp = PS[:, tu % 2, :]
                                    if tu >= 2:
                                        c_yev.wait(t, yid(e, ffh, th, tu - 2))
                                    for ffc in range(16):
                                        for jj in range(2):
                                            mm = t.matmul(
                                                pp[:, jj * 512:(jj + 1) * 512],
                                                hsl[:, ffc,
                                                    tu * 128:(tu + 1) * 128],
                                                w2buf[:, ffc,
                                                      jj * 512:(jj + 1) * 512],
                                                start=(ffc == 0), stop=(ffc == 15))
                                    c_mmy.inc(mm)

                def c_yev_expect(e, ffh, th):
                    q = e * 2 + ffh
                    idx = (q * 2 + th) - 1
                    pq, pth = divmod(idx, 2)
                    pe, pffh = divmod(pq, 2)
                    return yid(pe, pffh, pth, 7)

                def _sc(sc):
                    n = 0
                    for e in range(NEXP):
                        for ffh in range(2):
                            for th in range(2):
                                for ffc in range(16):
                                    n += 1
                                    c_mmh.wait(sc, n)
                                    if n >= 3:
                                        c_hev.wait(sc, n - 2)
                                    c_sil.inc(sc.activation(
                                        stmp[:, :],
                                        PS[:, ffc % 2, :], AF.Silu))

                def _v(v):
                    for e in range(NEXP):
                        for ffh in range(2):
                            for th in range(2):
                                for ffc in range(16):
                                    nn = hid(e, ffh, th, ffc)
                                    c_sil.wait(v, nn)
                                    c_hev.inc(v.tensor_mul(
                                        hsl[:, ffc, :], stmp[:, :],
                                        PS[:, 2 + ffc % 2, :]))
                                for tu in range(8):
                                    tt = th * 8 + tu
                                    c_mmy.wait(v, yid(e, ffh, th, tu))
                                    c_yev.inc(v.scalar_tensor_tensor(
                                        xo_f32[:, tt, :], PS[:, tu % 2, :],
                                        gates_e[:, e, tt:tt + 1],
                                        xo_f32[:, tt, :], ALU.mult, ALU.add))

                blk.gpsimd(_gp)
                blk.tensor(_t)
                blk.scalar(_sc)
                blk.vector(_v)

        with nc.Block() as blk:
            def _sy(sync):
                c_yev.wait(sync)
                c_st.dinc(sync.dma_start(out_d[:], xo_f32[:]))
                c_st.wait(sync, c_st.n)

            blk.sync(_sy)

        sc_moe.close()
        sc_y.close()
        sc_o.close()
        return nc, dram
    except _Cut:
        pass
    with nc.Block() as blk:
        def _v(v):
            v.memset(_stub["z"][:], 0.0)

        def _sy(g):
            cs = _stub["c"]
            for t in range(NT):
                cs.dinc(g.dma_start(out_d[:, t, :], _stub["z"][:]))
            cs.wait(g)

        blk.vector(_v)
        blk.gpsimd(_sy)
    return nc, dram


_BUILD_CACHE = {}


def kernel(**inputs):
    x = np.asarray(inputs["L0_attn_dense"], np.float32)
    rows = np.asarray(inputs["L0_attn_adj_rows"]).astype(np.int64)
    cols = np.asarray(inputs["L0_attn_adj_cols"]).astype(np.int64)
    vals = np.asarray(inputs["L0_attn_adj_vals"], np.float32)
    wq = np.asarray(inputs["L0_attn_wq"], np.float32)
    wk = np.asarray(inputs["L0_attn_wk"], np.float32)
    wv_w = np.asarray(inputs["L0_attn_wv_w"], np.float32)
    wv_b = np.asarray(inputs["L0_attn_wv_b"], np.float32)
    qkv_w = np.asarray(inputs["L0_attn_qkv_w"], np.float32)
    out_w = np.asarray(inputs["L0_attn_out_w"], np.float32)
    out_b = np.asarray(inputs["L0_attn_out_b"], np.float32)
    conv_w = np.asarray(inputs["L0_attn_conv_w"], np.float32)
    Wr = np.asarray(inputs["L0_ffn_Wr"], np.float32)
    W1 = np.asarray(inputs["L0_ffn_W1"], np.float32)
    W2 = np.asarray(inputs["L0_ffn_W2"], np.float32)
    W3 = np.asarray(inputs["L0_ffn_W3"], np.float32)

    AT, A2cT, Pm, ndp, CV, mpad = _dup_split(rows, cols, vals)
    CB = _conv_bands(conv_w).reshape(HEADS * 3 * 128, 128)
    wbig = np.ascontiguousarray(np.concatenate([qkv_w, wq, wk], axis=1))
    w1t = np.ascontiguousarray(W1.transpose(0, 2, 1))
    w3t = np.ascontiguousarray(W3.transpose(0, 2, 1))
    w2t = np.ascontiguousarray(W2.transpose(0, 2, 1))
    wrt = np.ascontiguousarray(Wr.T)
    outb_tile = np.ascontiguousarray(out_b.reshape(8, 128).T)
    m0m1 = np.zeros((128, 256), np.float32)
    m0m1[:, 0:64] = 1.0
    m0m1[:, 128 + 64:256] = 1.0
    eye = np.eye(128, dtype=np.float32)
    I256 = np.eye(256, dtype=np.float32)
    i2t = np.concatenate([I256[0:128, :], I256[128:256, :]], axis=1)
    c715 = np.concatenate([7.0 * i2t, 15.0 * i2t, 13.0 * i2t], axis=1)

    nc, dram = build(ndp, mpad, taps=())

    xs = x.reshape(NCORE, TOK, D)
    base = {
        "x0": np.ascontiguousarray(x[0]),
        "wbig": wbig, "wv": wv_w, "wvb": wv_b.reshape(1, D),
        "outw": out_w, "outb": outb_tile, "cb": CB, "wrt": wrt,
        "at": AT, "a2ct": A2cT, "pm": Pm, "cv": CV,
        "w1t": w1t, "w3t": w3t, "w2t": w2t,
        "identb": eye.astype(BF), "identf": eye,
        "onesb": np.ones((128, 128), BF), "onesf": np.ones((128, 128), np.float32),
        "m0m1": m0m1.astype(BF), "c715": c715,
    }
    in_maps = []
    for c in range(NCORE):
        m = dict(base)
        m["xown"] = np.ascontiguousarray(xs[c])
        hh = c % 2
        mk = np.zeros((128, 2), np.float32)
        mk[:, 0] = 1.0 - hh
        mk[:, 1] = hh
        m["mks"] = mk
        in_maps.append(m)

    res = run_bass_kernel_spmd(nc, in_maps, core_ids=list(range(NCORE)))
    outs = [r["out"].transpose(1, 0, 2).reshape(TOK, D) for r in res.results]
    return np.concatenate(outs, axis=0).reshape(B, S, D).astype(np.float32)
